# version 07 (final)
# speedup vs baseline: 1.0819x; 1.0054x over previous
import numpy as np
from contextlib import ExitStack

import concourse.bass as bass
import concourse.mybir as mybir
from concourse.bass_utils import run_bass_kernel_spmd
from concourse.alu_op_type import AluOpType

F32 = mybir.dt.float32
U32 = mybir.dt.uint32
AX = mybir.AxisListType
AF = mybir.ActivationFunctionType
MUL = AluOpType.mult
ADD = AluOpType.add
SUB = AluOpType.subtract

NCORES = 8
B = 16
S = 2048
D = 1024
NH = 16
HD = 64
RANK = 8
EPS_R = 1e-4
EPS_LN = 1e-5
ALPHA = 0.9
ATT_SCALE = 0.125
SLOC = S // NCORES
NTILE = B * SLOC // 128
NBATCH = 4
CSOL = 72
NKC = D // 128


class Ev:
    __slots__ = ("sem", "val")

    def __init__(self, sem, val):
        self.sem, self.val = sem, val


class P:
    def __init__(self, dbg=()):
        self.dbg = set(dbg)
        self.nc = bass.Bass("TRN2", target_bir_lowering=False)
        self.ctx = ExitStack()
        self.dbg_outs = {}
        self.cnt = {}
        self.sems = {}

    def _sem(self, name):
        if name not in self.sems:
            self.sems[name] = self.ctx.enter_context(self.nc.semaphore(name))
            self.cnt[name] = 0
        return self.sems[name]

    def mark(self, name, inst=None, by=1):
        sem = self._sem(name)
        self.cnt[name] += by
        if inst is not None:
            inst.then_inc(sem, by)
        return Ev(sem, self.cnt[name])

    def last(self, name):
        return Ev(self._sem(name), self.cnt[name])

    @staticmethod
    def wait(eng, ev):
        if ev is not None:
            eng.wait_ge(ev.sem, ev.val)

    def sbuf(self, name, shape, dtype=F32, ctx=None):
        return (ctx or self.ctx).enter_context(self.nc.sbuf_tensor(name, shape, dtype))

    def psum(self, name, shape, dtype=F32, ctx=None):
        return (ctx or self.ctx).enter_context(self.nc.psum_tensor(name, shape, dtype))

    def dram(self, name, shape, dtype=F32, kind=None):
        if kind is None:
            return self.nc.dram_tensor(name, shape, dtype)
        return self.nc.dram_tensor(name, shape, dtype, kind=kind)

    def tap(self, name, shape, dtype=F32):
        if name in self.dbg:
            t = self.nc.dram_tensor("dbg_" + name, shape, dtype, kind="ExternalOutput")
            self.dbg_outs[name] = t.ap()
            return self.dbg_outs[name]
        return None


def build(dbg=(), stop_after=None):
    p = P(dbg)
    nc = p.nc
    with p.ctx:
        x_in = p.dram("x_in", [B, SLOC, D], kind="ExternalInput").ap()
        W = {}
        for w in ["wq", "wk", "wv", "woA",
                  "wfT", "ufT", "wiT", "uiT", "wcT", "ucT", "woT", "uoT"]:
            W[w] = p.dram(w, [D, D], kind="ExternalInput").ap()
        ln_g = p.dram("ln_g", [1, D], kind="ExternalInput").ap()
        ln_b = p.dram("ln_b", [1, D], kind="ExternalInput").ap()
        bias_f = p.dram("bias_f", [128, 5 * NKC], kind="ExternalInput").ap()
        ident_in = p.dram("ident_in", [128, 128], kind="ExternalInput").ap()
        cconst = p.dram("cconst", [128, 4], kind="ExternalInput").ap()
        bias_rows = p.dram("bias_rows", [1, 4, D], kind="ExternalInput").ap()
        pfx_idx = p.dram("pfx_idx", [1, 8], U32, kind="ExternalInput").ap()
        pfx_mask = p.dram("pfx_mask", [128, 8], kind="ExternalInput").ap()
        out = p.dram("out", [B, SLOC, D], kind="ExternalOutput").ap()

        z_st = p.dram("z_st", [NTILE, 128, D]).ap()
        q_st = p.dram("q_st", [NTILE, 128, NKC, 128]).ap()
        g_st = p.dram("g_st", [NBATCH, 64, 64 * 128]).ap()
        b8_st = p.dram("b8_st", [NBATCH, 64, 8 * 128]).ap()
        b_st = p.dram("b_st", [NBATCH, 64, 128 * 64]).ap()
        bt_st = p.dram("bt_st", [NBATCH, 64, 128 * 64]).ap()
        xmid_st = p.dram("xmid_st", [NTILE, 128, D]).ap()
        z2f_st = p.dram("z2f_st", [NTILE, 128, NKC, 128]).ap()
        gz_st = p.dram("gz_st", [NTILE, 128, 3, D]).ap()
        pfx_cin = p.dram("pfx_cin", [4, 128, 2048])

        idt = p.sbuf("idt", [128, 128])
        ccon = p.sbuf("ccon", [128, 4])
        pfxi = p.sbuf("pfxi", [1, 8], U32)
        pmask = p.sbuf("pmask", [128, 8])
        biasf = p.sbuf("biasf", [128, 5 * NKC])

        blk = p.ctx.enter_context(nc.Block())

        @blk.sync
        def _(sync):
            p.mark("dsy", sync.dma_start(out=idt[:], in_=ident_in), 16)
            p.mark("dsy", sync.dma_start(out=ccon[:], in_=cconst), 16)
            p.mark("dsy", sync.dma_start(out=pfxi[:], in_=pfx_idx), 16)
            p.mark("dsy", sync.dma_start(out=pmask[:], in_=pfx_mask), 16)
            p.mark("dsy", sync.dma_start(out=biasf[:], in_=bias_f), 16)

        p1c = ExitStack()
        g_bc = p.sbuf("g_bc", [128, NKC, 128], ctx=p1c)
        b_bc = p.sbuf("b_bc", [128, NKC, 128], ctx=p1c)
        wA = p.sbuf("wA", [128, NKC, D], ctx=p1c)
        wB = p.sbuf("wB", [128, NKC, D], ctx=p1c)
        wC = p.sbuf("wC", [128, NKC, D], ctx=p1c)
        xt = p.sbuf("xt", [128, 2, D], ctx=p1c)
        zt = p.sbuf("zt", [128, 2, D], ctx=p1c)
        znt = p.sbuf("znt", [128, 2, D], ctx=p1c)
        znf = p.sbuf("znf", [128, NKC, 4, 128], ctx=p1c)
        kt = p.sbuf("kt", [128, D], ctx=p1c)
        vt = p.sbuf("vt", [128, D], ctx=p1c)
        bankimg = p.sbuf("bankimg", [128, 4, 384], ctx=p1c)
        bankimg2 = p.sbuf("bankimg2", [128, 72, 8], ctx=p1c)
        qstg = p.sbuf("qstg", [128, NKC, 4, 128], ctx=p1c)
        stat = p.sbuf("stat", [128, 16], ctx=p1c)
        sqs = p.sbuf("sqs", [128, D], ctx=p1c)
        sac = p.sbuf("sac", [128, 4], ctx=p1c)

        ps_tr = p.psum("ps_tr", [128, 512], ctx=p1c)
        ps_kv = p.psum("ps_kv", [128, 1024], ctx=p1c)
        ps_g = p.psum("ps_g", [128, 2, 512], ctx=p1c)
        ps_q = p.psum("ps_q", [128, 1024], ctx=p1c)

        @blk.sync
        def _(sync):
            p.mark("dsy", sync.dma_start(out=g_bc[:],
                in_=ln_g.rearrange("o (c f) -> o c f", f=128)
                .to_broadcast([128, NKC, 128])), 16)
            p.mark("dsy", sync.dma_start(out=b_bc[:],
                in_=ln_b.rearrange("o (c f) -> o c f", f=128)
                .to_broadcast([128, NKC, 128])), 16)
            p.mark("dsy", sync.dma_start(out=wA[:],
                in_=W["wk"].rearrange("(kc q) d -> q kc d", q=128)), 16)
            p.mark("dsy", sync.dma_start(out=wB[:],
                in_=W["wv"].rearrange("(kc q) d -> q kc d", q=128)), 16)
            p.mark("dsy", sync.dma_start(out=wC[:],
                in_=W["wq"].rearrange("(kc q) d -> q kc d", q=128)), 16)
        ev_w = p.last("dsy")

        tap_z = p.tap("z", [NTILE, 128, D])
        tap_x0 = p.tap("x0", [128, D])
        tap_stat = p.tap("stat", [128, 16])
        tap_zn = p.tap("zn", [NTILE, 128, D])
        tap_kv = p.tap("kv", [2, 128, D])
        tap_gbt = p.tap("gbt", [3, 64, 128 * 64])
        tap_b8 = p.tap("b8", [64, 8 * 128])
        tap_q = p.tap("q", [NTILE, 128, NKC, 128])

        def make_ln(stat, sqs, sac):
          def ln_pass(i, src_ap, dst_ap, c0, gate_v=None, gate_s=None):
              C = lambda k: stat[:, c0 + k:c0 + k + 1]

              @blk.vector
              def _(vector):
                  p.wait(vector, gate_v)
                  vector.tensor_reduce(C(0), src_ap, AX.X, ADD)
                  vector.tensor_copy(sac[:, 0:1], C(0))
                  iv = vector.tensor_copy(sac[:, 0:1], C(0))
                  p.mark("dve", iv)

              @blk.scalar
              def _(scalar):
                  p.wait(scalar, gate_s)
                  scalar.activation(sqs[:], src_ap, AF.Square, accum_out=C(1))
                  scalar.copy(sac[:, 1:2], C(1))
                  iv = scalar.copy(sac[:, 1:2], C(1))
                  p.mark("sc", iv)
              e1 = p.last("sc")

              @blk.vector
              def _(vector):
                  p.wait(vector, e1)
                  vector.tensor_scalar_mul(C(2), C(0), 1.0 / D)
                  vector.tensor_copy(sac[:, 0:1], C(0))
                  vector.tensor_mul(C(3), C(2), C(2))
                  vector.tensor_copy(sac[:, 1:2], C(0))
                  vector.scalar_tensor_tensor(C(4), C(1), 1.0 / D, C(3), MUL, SUB)
                  vector.tensor_copy(sac[:, 2:3], C(0))
                  vector.tensor_scalar_add(C(4), C(4), EPS_LN)
                  iv = vector.tensor_copy(sac[:, 3:4], C(0))
                  p.mark("dve", iv)
              e2 = p.last("dve")

              @blk.scalar
              def _(scalar):
                  p.wait(scalar, e2)
                  iv = scalar.sqrt(C(5), C(4))
                  p.mark("sc", iv)
              e3 = p.last("sc")

              @blk.vector
              def _(vector):
                  p.wait(vector, e3)
                  vector.reciprocal(C(6), C(5))
                  vector.tensor_copy(sac[:, 0:1], C(5))
                  vector.scalar_tensor_tensor(C(7), C(2), -1.0, C(6), MUL, MUL)
                  iv = vector.tensor_copy(sac[:, 1:2], C(5))
                  p.mark("dve", iv)
              e4 = p.last("dve")

              @blk.scalar
              def _(scalar):
                  p.wait(scalar, e4)
                  iv = scalar.activation(dst_ap, src_ap, AF.Identity,
                                         bias=C(7), scale=C(6))
                  p.mark("sc", iv)
              return p.last("sc")
          return ln_pass

        def make_ln_pair(stat, sqs, sac):
          def ln_pair(src3, dst3, c0, gate_v=None, gate_s=None, gate_apply=None):
              C = lambda k: stat[:, c0 + 2 * k:c0 + 2 * k + 2]
              Ct = lambda k, ti: stat[:, c0 + 2 * k + ti:c0 + 2 * k + ti + 1]

              @blk.vector
              def _(vector):
                  p.wait(vector, gate_v)
                  vector.tensor_reduce(Ct(0, 0), src3[:, 0, :], AX.X, ADD)
                  vector.tensor_reduce(Ct(0, 1), src3[:, 1, :], AX.X, ADD)
                  vector.tensor_copy(sac[:, 0:2], C(0))
                  iv = vector.tensor_copy(sac[:, 0:2], C(0))
                  p.mark("dve", iv)
              e_red = p.last("dve")

              @blk.scalar
              def _(scalar):
                  p.wait(scalar, gate_s)
                  p.wait(scalar, e_red)
                  scalar.activation(sqs[:], src3[:, 0, :], AF.Square,
                                    accum_out=Ct(1, 0))
                  scalar.activation(sqs[:], src3[:, 1, :], AF.Square,
                                    accum_out=Ct(1, 1))
                  scalar.copy(sac[:, 2:4], C(1))
                  scalar.copy(sac[:, 2:4], C(1))
                  iv = scalar.copy(sac[:, 2:4], C(1))
                  p.mark("sc", iv)
              e1 = p.last("sc")

              @blk.gpsimd
              def _(gpsimd):
                  p.wait(gpsimd, e1)
                  p.mark("gnp", gpsimd.engine_nop())
              e1g = p.last("gnp")

              @blk.vector
              def _(vector):
                  p.wait(vector, e1g)
                  vector.tensor_scalar_mul(C(2), C(0), 1.0 / D)
                  vector.tensor_copy(sac[:, 0:2], C(0))
                  vector.tensor_copy(sac[:, 2:4], C(0))
                  vector.tensor_mul(C(3), C(2), C(2))
                  vector.tensor_copy(sac[:, 2:4], C(0))
                  vector.tensor_copy(sac[:, 0:2], C(0))
                  vector.scalar_tensor_tensor(C(4), C(1), 1.0 / D, C(3), MUL, SUB)
                  vector.tensor_copy(sac[:, 0:2], C(0))
                  vector.tensor_copy(sac[:, 2:4], C(0))
                  vector.tensor_scalar_add(C(4), C(4), EPS_LN)
                  vector.tensor_copy(sac[:, 2:4], C(0))
                  iv = vector.tensor_copy(sac[:, 0:2], C(0))
                  p.mark("dve", iv)
              e2 = p.last("dve")

              @blk.gpsimd
              def _(gpsimd):
                  p.wait(gpsimd, e2)
                  p.mark("gnp", gpsimd.engine_nop())
              e2g = p.last("gnp")

              @blk.scalar
              def _(scalar):
                  p.wait(scalar, e2g)
                  iv = scalar.sqrt(C(5), C(4))
                  p.mark("sc", iv)
              e3 = p.last("sc")

              @blk.vector
              def _(vector):
                  p.wait(vector, e3)
                  vector.reciprocal(C(6), C(5))
                  vector.tensor_copy(sac[:, 0:2], C(5))
                  vector.tensor_copy(sac[:, 2:4], C(5))
                  vector.scalar_tensor_tensor(C(7), C(2), -1.0, C(6), MUL, MUL)
                  vector.tensor_copy(sac[:, 2:4], C(5))
                  iv = vector.tensor_copy(sac[:, 0:2], C(5))
                  p.mark("dve", iv)
              e4 = p.last("dve")

              @blk.gpsimd
              def _(gpsimd):
                  p.wait(gpsimd, e4)
                  p.mark("gnp", gpsimd.engine_nop())
              e4g = p.last("gnp")

              @blk.scalar
              def _(scalar):
                  p.wait(scalar, e4g)
                  p.wait(scalar, gate_apply)
                  for ti in range(2):
                      iv = scalar.activation(dst3[:, ti, :], src3[:, ti, :],
                                             AF.Identity,
                                             bias=Ct(7, ti), scale=Ct(6, ti))
                  p.mark("sc", iv)
              return p.last("sc")
          return ln_pair

        ln_pass = make_ln(stat, sqs, sac)

        ev_x = [None] * NTILE
        ev_ln1ap = [None] * NTILE
        ev_ln2ap = [None] * NTILE
        ev_ln = [None] * NTILE
        ev_zst = [None] * NTILE
        ev_tr = [None] * NTILE
        ev_gev = [[None] * 4 for _ in range(NTILE)]
        ev_gst = [None] * NTILE
        ev_qmm = [None] * (NTILE // 4)
        ev_qst = [None] * (NTILE // 4)


        def p1_front(i):
            b, cs = i // 2, i % 2
            bat, til = i // 8, i % 8
            r = i % 2
            @blk.sync
            def _(sync, i=i, b=b, cs=cs, r=r):
                if i >= 2:
                    p.wait(sync, ev_ln1ap[i - 2])
                ld = sync.dma_start(out=xt[:, r, :],
                                    in_=x_in[b, cs * 128:(cs + 1) * 128, :])
                ev_x[i] = p.mark("dsy", ld, 16)

            @blk.vector
            def _(vector, i=i):
                p.wait(vector, ev_x[i])
                if i == 0:
                    p.wait(vector, ev_w)
                vector.engine_nop()

            if i >= 2:
                @blk.scalar
                def _(scalar, i=i):
                    p.wait(scalar, ev_zst[i - 2])
            ev_ln1ap[i] = ln_pass(i, xt[:, r, :], zt[:, r, :], 0, gate_v=ev_x[i], gate_s=ev_x[i])

            if i >= 2:
                @blk.scalar
                def _(scalar, i=i):
                    p.wait(scalar, ev_tr[i - 2])
            ev_ln2ap[i] = ln_pass(i, zt[:, r, :], znt[:, r, :], 8, gate_v=ev_ln1ap[i])

            @blk.vector
            def _(vector, i=i, r=r):
                p.wait(vector, ev_ln2ap[i])
                zn3 = znt[:, r, :].rearrange("q (c f) -> q c f", f=128)
                vector.tensor_mul(zn3, zn3, g_bc[:])
                iv = vector.tensor_add(zn3, zn3, b_bc[:])
                ev_ln[i] = p.mark("dve", iv)

            @blk.gpsimd
            def _(gpsimd, i=i, r=r):
                p.wait(gpsimd, ev_ln[i])
                if i == 0 and tap_x0 is not None:
                    p.mark("dgp", gpsimd.dma_start(out=tap_x0, in_=xt[:, 0, :]), 16)
                if i == 0 and tap_stat is not None:
                    p.mark("dgp", gpsimd.dma_start(out=tap_stat, in_=stat[:]), 16)
                ld = gpsimd.dma_start(out=z_st[i], in_=zt[:, r, :])
                ev_zst[i] = p.mark("dgp", ld, 16)
                if tap_z is not None:
                    p.mark("dgp", gpsimd.dma_start(out=tap_z[i], in_=zt[:, r, :]), 16)
                if tap_zn is not None:
                    p.mark("dgp", gpsimd.dma_start(out=tap_zn[i], in_=znt[:, r, :]), 16)


        def p1_back(i):
            b, cs = i // 2, i % 2
            bat, til = i // 8, i % 8
            r = i % 2
            for half in range(2):
                @blk.tensor
                def _(tensor, i=i, r=r, half=half):
                    if half == 0:
                        p.wait(tensor, ev_ln[i])
                    if i > 0 or half > 0:
                        p.wait(tensor, p.last("trev"))
                    for c4 in range(4):
                        ch = half * 4 + c4
                        iv = tensor.transpose(ps_tr[:, c4 * 128:(c4 + 1) * 128],
                                              znt[:, r, ch * 128:(ch + 1) * 128],
                                              idt[:])
                    p.mark("pe", iv)
                eh = p.last("pe")

                @blk.scalar
                def _(scalar, i=i, half=half, eh=eh):
                    p.wait(scalar, eh)
                    iv = scalar.copy(znf[:, half * 4:(half + 1) * 4, i % 4, :],
                                     ps_tr[:].rearrange("q (c f) -> q c f", f=128))
                    p.mark("trev", iv)
            ev_tr[i] = p.last("pe")
            ev_znf = p.last("trev")

            for n4, (wsb, dst, hf) in enumerate([(wA, kt, 0), (wA, kt, 1),
                                                 (wB, vt, 0), (wB, vt, 1)]):
                bank = n4 % 2

                @blk.tensor
                def _(tensor, i=i, wsb=wsb, hf=hf, n4=n4, bank=bank, ev_znf=ev_znf):
                    if n4 == 0:
                        p.wait(tensor, ev_znf)
                    if i > 0 or n4 >= 2:
                        p.wait(tensor, Ev(p._sem("kvev"), p.cnt["kvev"] - 1))
                    for kc in range(NKC):
                        iv = tensor.matmul(ps_kv[:, bank * 512:(bank + 1) * 512],
                                           znf[:, kc, i % 4, :],
                                           wsb[:, kc, hf * 512:(hf + 1) * 512],
                                           start=(kc == 0), stop=(kc == NKC - 1))
                    p.mark("pe", iv)
                evm = p.last("pe")

                @blk.scalar
                def _(scalar, dst=dst, hf=hf, bank=bank, evm=evm):
                    p.wait(scalar, evm)
                    iv = scalar.copy(dst[:, hf * 512:(hf + 1) * 512],
                                     ps_kv[:, bank * 512:(bank + 1) * 512])
                    p.mark("kvev", iv)

            if tap_kv is not None and i == 0:
                @blk.gpsimd
                def _(gpsimd):
                    p.wait(gpsimd, p.last("kvev"))
                    p.mark("dgp", gpsimd.dma_start(out=tap_kv[0], in_=kt[:]), 16)
                    p.mark("dgp", gpsimd.dma_start(out=tap_kv[1], in_=vt[:]), 16)
            ev_kvev = p.last("kvev")

            for rnd in range(4):
                @blk.tensor
                def _(tensor, i=i, rnd=rnd, ev_kvev=ev_kvev):
                    if rnd == 0:
                        p.wait(tensor, ev_kvev)
                    prev = ev_gev[i][rnd - 2] if rnd >= 2 else (
                        ev_gev[i - 1][rnd + 2] if i > 0 else None)
                    p.wait(tensor, prev)
                    for pp in range(2):
                        for vert in range(2):
                            h = rnd * 4 + pp * 2 + vert
                            kh = kt[:, h * 64:(h + 1) * 64]
                            vh = vt[:, h * 64:(h + 1) * 64]
                            tp = (0, 64) if vert else (0, 0)
                            sl = slice(vert * 64, vert * 64 + 64)
                            o = pp * 192
                            tensor.matmul(ps_g[sl, rnd % 2, o:o + 64], kh, kh,
                                          start=True, stop=True, tile_position=tp)
                            tensor.matmul(ps_g[sl, rnd % 2, o + 64:o + 128], kh, vh,
                                          start=True, stop=True, tile_position=tp)
                            iv = tensor.matmul(ps_g[sl, rnd % 2, o + 128:o + 192],
                                               vh, kh, start=True, stop=True,
                                               tile_position=tp)
                    p.mark("pe", iv)
                eg = p.last("pe")

                @blk.vector
                def _(vector, i=i, rnd=rnd, eg=eg):
                    p.wait(vector, eg)
                    if i > 0:
                        p.wait(vector, ev_gst[i - 1])
                    vector.tensor_copy(bankimg[:, rnd, :], ps_g[:, rnd % 2, 0:384])
                    psl = ps_g[:, rnd % 2, 0:384].rearrange("q (pp m j) -> q pp m j",
                                                            pp=2, m=3)
                    vector.tensor_copy(bankimg2[:, 0:64, rnd * 2:rnd * 2 + 2],
                                       psl[:, :, 0, :].transpose([0, 2, 1]))
                    iv = vector.tensor_copy(bankimg2[:, 64:72, rnd * 2:rnd * 2 + 2],
                                            psl[:, :, 1, 0:8].transpose([0, 2, 1]))
                    ev_gev[i][rnd] = p.mark("gev", iv)

            @blk.gpsimd
            def _(gpsimd, i=i, bat=bat, til=til):
                p.wait(gpsimd, ev_gev[i][3])
                srcv = bankimg[:].rearrange("q r (pp m j) -> q r pp m j", pp=2, m=3)
                for vert in range(2):
                    sl = slice(vert * 64, vert * 64 + 64)
                    gdst = g_st[bat].rearrange("a (j t v g) -> a j t v g",
                                               t=8, v=2, g=8)[:, :, til, vert, :]
                    p.mark("dgp", gpsimd.dma_start(
                        out=gdst, in_=bankimg2[sl, 0:64, :]), 16)
                    b8dst = b8_st[bat].rearrange("a (j t v g) -> a j t v g",
                                                 j=8, t=8, v=2, g=8)[:, :, til, vert, :]
                    p.mark("dgp", gpsimd.dma_start(
                        out=b8dst, in_=bankimg2[sl, 64:72, :]), 16)
                    for mat, dstt in [(1, b_st), (2, bt_st)]:
                        dd = dstt[bat].rearrange("a (t v g j) -> a t v g j",
                                                 t=8, v=2, g=8)[:, til, vert, :, :]
                        p.mark("dgp", gpsimd.dma_start(
                            out=dd.rearrange("a (r pp) j -> a r pp j", r=4),
                            in_=srcv[sl, :, :, mat, :]), 16)
                ev_gst[i] = p.last("dgp")

            if i % 4 == 3:
                span = i // 4
                for rnd in range(8):
                    bank = rnd % 2

                    @blk.tensor
                    def _(tensor, span=span, rnd=rnd, bank=bank):
                        if span > 0 or rnd >= 2:
                            p.wait(tensor, Ev(p._sem("qev"), p.cnt["qev"] - 1))
                        for kc in range(NKC):
                            iv = tensor.matmul(
                                ps_q[:, bank * 512:(bank + 1) * 512],
                                wC[:, kc, rnd * 128:(rnd + 1) * 128],
                                znf[:, kc, :, :],
                                start=(kc == 0), stop=(kc == NKC - 1))
                        p.mark("pe", iv)
                    eq = p.last("pe")

                    @blk.vector
                    def _(vector, span=span, rnd=rnd, bank=bank, eq=eq):
                        p.wait(vector, eq)
                        if rnd == 0 and span > 0:
                            p.wait(vector, ev_qst[span - 1])
                        iv = vector.tensor_copy(
                            qstg[:, rnd, :, :],
                            ps_q[:, bank * 512:(bank + 1) * 512]
                            .rearrange("q (t f) -> q t f", t=4))
                        p.mark("qev", iv)
                ev_qmm[span] = p.last("pe")

                @blk.gpsimd
                def _(gpsimd, span=span):
                    p.wait(gpsimd, p.last("qev"))
                    iv = gpsimd.dma_start(
                        out=q_st[span * 4:span * 4 + 4].transpose([1, 2, 0, 3]),
                        in_=qstg[:])
                    ev_qst[span] = p.mark("dgp", iv, 16)
                    if tap_q is not None:
                        p.mark("dgp", gpsimd.dma_start(
                            out=tap_q[span * 4:span * 4 + 4].transpose([1, 2, 0, 3]),
                            in_=qstg[:]), 16)
                        ev_qst[span] = p.last("dgp")

            if tap_gbt is not None and i == 0:
                @blk.gpsimd
                def _(gpsimd):
                    p.wait(gpsimd, ev_gst[0])
                    for m, srcd in [(0, g_st), (1, b_st), (2, bt_st)]:
                        p.mark("dgp", gpsimd.dma_start(
                            out=tap_gbt[m], in_=srcd[0][:, :]), 16)
                    p.mark("dgp", gpsimd.dma_start(
                        out=tap_b8, in_=b8_st[0][:, :]), 16)


        for i in range(NTILE + 1):
            if i < NTILE:
                p1_front(i)
            if i >= 1:
                p1_back(i - 1)

        p1c.close()

        if stop_after == "p1":
            _finish(p, blk)
            return p, out

        p24c = ExitStack()
        wr0 = p.sbuf("wr0", [128, 8192], ctx=p24c)
        wr1 = p.sbuf("wr1", [128, 8192], ctx=p24c)

        p2c = ExitStack()
        g_sb = p.sbuf("g_sb", [64, 16 * 128], ctx=p2c)
        b8_sb = p.sbuf("b8_sb", [64, 8 * 128], ctx=p2c)
        bh = p.sbuf("bh", [64, 64 * 64], ctx=p2c)
        bth = p.sbuf("bth", [64, 64 * 64], ctx=p2c)
        A = p.sbuf("A", [128, 64 * CSOL], ctx=p2c)
        scr = p.sbuf("scr", [128, 4608], ctx=p2c)
        rd = p.sbuf("rd", [128, 64], ctx=p2c)
        ype = p.sbuf("ype", [64, 1024], ctx=p2c)
        rpe = p.sbuf("rpe", [64, 1024], ctx=p2c)
        zsb = p.sbuf("zsb", [64, 1024], ctx=p2c)
        qdve = p.sbuf("qdve", [128, 512], ctx=p2c)
        qtf = p.sbuf("qtf", [8, 8192], ctx=p2c)
        fgrp = p.sbuf("fgrp", [8, 2, 512], ctx=p2c)
        cD = p.sbuf("cD", [128, 2048], ctx=p2c)
        st2 = p.sbuf("st2", [128, 16], ctx=p2c)
        sac2 = p.sbuf("sac2", [128, 4], ctx=p2c)

        ps_br = p.psum("ps_br", [128, 2, 512], ctx=p2c)
        ps_z = p.psum("ps_z", [64, 512], ctx=p2c)
        ps_f = p.psum("ps_f", [8, 512], ctx=p2c)
        ps_wr = p.psum("ps_wr", [128, 2, 512], ctx=p2c)

        A3 = A[:].rearrange("q (a c) -> q a c", c=CSOL)
        arena = p.dram("arena", [4, 8 * 128, 2048])

        tap_A = p.tap("A", [NBATCH, 128, 64 * CSOL])
        tap_rpe = p.tap("rpe", [NBATCH, 64, 1024])
        tap_qcgs = p.tap("qcgs", [NBATCH, 128, 512])
        tap_wr = p.tap("wrs", [2, 128, 8192])
        tap_cd = p.tap("cd", [NBATCH, 128, 2048])
        tap_sab = p.tap("sab", [2, 128, 8192])

        def sacv(vector):
            vector.tensor_copy(sac2[:, 0:1], rd[:, 0:1])

        def bridge_cols_to_A(cols, src_ap_of_j, dst_col_of_j, ev_src):
            for g0 in range(0, len(cols), 8):
                grp = cols[g0:g0 + 8]
                par = (g0 // 8) % 2

                @blk.tensor
                def _(tensor, grp=grp, par=par, g0=g0, ev_src=ev_src):
                    if g0 == 0:
                        p.wait(tensor, ev_src)
                    p.wait(tensor, p.last("brev"))
                    for n, j in enumerate(grp):
                        iv = tensor.transpose(ps_br[:, par, n * 64:(n + 1) * 64],
                                              src_ap_of_j(j), idt[0:64, 0:64])
                    p.mark("pe", iv)
                ebr = p.last("pe")

                @blk.vector
                def _(vector, grp=grp, par=par, ebr=ebr):
                    p.wait(vector, ebr)
                    for n, j in enumerate(grp):
                        iv = vector.tensor_copy(A3[:, :, dst_col_of_j(j)],
                                                ps_br[:, par, n * 64:(n + 1) * 64])
                    p.mark("brev", iv)
            return p.last("brev")

        def back_sub(vector):
            for i in range(63, -1, -1):
                xrow = A3[:, i, 64:CSOL]
                vector.tensor_scalar_mul(xrow, xrow, rd[:, i:i + 1])
                sacv(vector)
                if i == 0:
                    continue
                so = scr[:, 0:i * 8].rearrange("q (a r) -> q a r", r=8)
                ucol = A3[:, 0:i, i].unsqueeze(2).to_broadcast([128, i, 8])
                xbc = xrow.unsqueeze(1).to_broadcast([128, i, 8])
                vector.tensor_tensor(so, ucol, xbc, MUL)
                if i < 8:
                    sacv(vector)
                vector.tensor_sub(A3[:, 0:i, 64:CSOL], A3[:, 0:i, 64:CSOL], so)
                if i < 8:
                    sacv(vector)

        def fwd_sub(vector):
            for k in range(0, 63):
                n_i = 63 - k
                yrow = A3[:, k, 64:CSOL]
                lcol = A3[:, k + 1:, k].unsqueeze(2).to_broadcast([128, n_i, 8])
                ybc = yrow.unsqueeze(1).to_broadcast([128, n_i, 8])
                so = scr[:, 0:n_i * 8].rearrange("q (a r) -> q a r", r=8)
                vector.tensor_tensor(so, lcol, ybc, MUL)
                if n_i < 8:
                    sacv(vector)
                vector.tensor_sub(A3[:, k + 1:, 64:CSOL], A3[:, k + 1:, 64:CSOL], so)
                if n_i < 8:
                    sacv(vector)

        def bridge_rhs_out(dst, ev_gate):
            for r in range(8):
                par = r % 2

                @blk.tensor
                def _(tensor, r=r, par=par, ev_gate=ev_gate):
                    if r == 0:
                        p.wait(tensor, ev_gate)
                    p.wait(tensor, p.last("brev"))
                    iv = tensor.transpose(ps_br[0:64, par, 0:128],
                                          A3[:, :, 64 + r], idt[:])
                    p.mark("pe", iv)
                et = p.last("pe")

                @blk.vector
                def _(vector, r=r, par=par, et=et, dst=dst):
                    p.wait(vector, et)
                    iv = vector.tensor_copy(dst[:, r:1024:8],
                                            ps_br[0:64, par, 0:128])
                    p.mark("brev", iv)
            return p.last("brev")

        def bridge_rhs_in(src, ev_gate):
            for r in range(8):
                par = r % 2

                @blk.tensor
                def _(tensor, r=r, par=par, ev_gate=ev_gate):
                    if r == 0:
                        p.wait(tensor, ev_gate)
                    p.wait(tensor, p.last("brev"))
                    iv = tensor.transpose(ps_br[:, par, 0:64],
                                          src[:, r:1024:8], idt[0:64, 0:64])
                    p.mark("pe", iv)
                et = p.last("pe")

                @blk.vector
                def _(vector, r=r, par=par, et=et):
                    p.wait(vector, et)
                    iv = vector.tensor_copy(A3[:, :, 64 + r],
                                            ps_br[:, par, 0:64])
                    p.mark("brev", iv)
            return p.last("brev")

        for bat in range(NBATCH):
            @blk.sync
            def _(sync, bat=bat):
                if bat == 0:
                    for nm in ["pe", "dve", "sc", "gev", "kvev", "trev", "qev",
                               "dgp", "dsy"]:
                        if nm in p.sems:
                            sync.wait_ge(p.sems[nm], p.cnt[nm])
                else:
                    p.wait(sync, p.last("brev"))
                p.mark("dsy", sync.dma_start(out=g_sb[:],
                                             in_=g_st[bat][:, 0:2048]), 16)
                p.mark("dsy", sync.dma_start(out=b8_sb[:], in_=b8_st[bat]), 16)
            ev_g0 = p.last("dsy")

            @blk.gpsimd
            def _(gpsimd, bat=bat):
                if bat > 0:
                    p.wait(gpsimd, p.last("fev"))
                p.mark("dgp", gpsimd.dma_start(out=bth[:],
                                               in_=bt_st[bat][:, 0:4096]), 16)
                p.mark("dgp", gpsimd.dma_start(out=bh[:],
                                               in_=b_st[bat][:, 0:4096]), 16)
            ev_bbt0 = p.last("dgp")

            for jr in range(4):
                if jr > 0:
                    @blk.sync
                    def _(sync, bat=bat, jr=jr):
                        p.wait(sync, p.last("brev"))
                        p.mark("dsy", sync.dma_start(
                            out=g_sb[:],
                            in_=g_st[bat][:, jr * 2048:(jr + 1) * 2048]), 16)
                    ev_g0 = p.last("dsy")
                bridge_cols_to_A(
                    list(range(16)),
                    lambda j: g_sb[:, j * 128:(j + 1) * 128],
                    lambda j, jr=jr: jr * 16 + j,
                    ev_g0)
            ev_b8 = bridge_cols_to_A(
                list(range(8)),
                lambda r: b8_sb[:, r * 128:(r + 1) * 128],
                lambda r: 64 + r,
                p.last("dsy"))

            @blk.vector
            def _(vector, ev_b8=ev_b8):
                p.wait(vector, ev_b8)
                diag = A[:, 0:63 * (CSOL + 1) + 1:CSOL + 1]
                vector.tensor_scalar_add(diag, diag, EPS_R)
                sacv(vector)
                for j in range(64):
                    tail = (63 - j) * (71 - j) < 500
                    n_i = 63 - j
                    vector.reciprocal(rd[:, j:j + 1], A3[:, j, j:j + 1])
                    sacv(vector)
                    if n_i == 0:
                        continue
                    mcol = A3[:, j + 1:, j]
                    vector.tensor_scalar_mul(mcol, mcol, rd[:, j:j + 1])
                    sacv(vector)
                    n_c = CSOL - 1 - j
                    so = scr[:, 0:n_i * n_c].rearrange("q (a c) -> q a c", c=n_c)
                    colbc = A3[:, j + 1:, j].unsqueeze(2).to_broadcast([128, n_i, n_c])
                    rowbc = A3[:, j, j + 1:CSOL].unsqueeze(1) \
                        .to_broadcast([128, n_i, n_c])
                    vector.tensor_tensor(so, colbc, rowbc, MUL)
                    if tail:
                        sacv(vector)
                    vector.tensor_sub(A3[:, j + 1:, j + 1:CSOL],
                                      A3[:, j + 1:, j + 1:CSOL], so)
                    if tail:
                        sacv(vector)
                back_sub(vector)
                iv = vector.tensor_copy(sac2[:, 1:2], rd[:, 0:1])
                p.mark("dve", iv)
            ev_r1 = p.last("dve")

            if tap_A is not None:
                @blk.gpsimd
                def _(gpsimd, bat=bat, ev_r1=ev_r1):
                    p.wait(gpsimd, ev_r1)
                    p.mark("dgp", gpsimd.dma_start(out=tap_A[bat], in_=A[:]), 16)

            ev_ype = bridge_rhs_out(ype, ev_r1)
            for half in range(2):
                @blk.tensor
                def _(tensor, bat=bat, half=half, ev_ype=ev_ype, ev_bbt0=ev_bbt0):
                    p.wait(tensor, ev_ype)
                    p.wait(tensor, ev_bbt0 if half == 0 else p.last("dgp"))
                    if half == 1:
                        p.wait(tensor, p.last("zev"))
                    for ql in range(64):
                        q = half * 64 + ql
                        iv = tensor.matmul(ps_z[0:64, ql * 8:(ql + 1) * 8],
                                           bth[:, ql * 64:(ql + 1) * 64],
                                           ype[:, q * 8:(q + 1) * 8],
                                           start=True, stop=True)
                    p.mark("pe", iv)
                ez = p.last("pe")

                @blk.vector
                def _(vector, half=half, ez=ez):
                    p.wait(vector, ez)
                    iv = vector.tensor_copy(zsb[:, half * 512:(half + 1) * 512],
                                            ps_z[0:64, :])
                    p.mark("zev", iv)

                if half == 0:
                    @blk.gpsimd
                    def _(gpsimd, bat=bat):
                        p.wait(gpsimd, p.last("zev"))
                        p.mark("dgp", gpsimd.dma_start(
                            out=bth[:], in_=bt_st[bat][:, 4096:8192]), 16)
            ev_zsb = p.last("zev")
            ev_zin = bridge_rhs_in(zsb, ev_zsb)

            @blk.vector
            def _(vector, ev_zin=ev_zin):
                p.wait(vector, ev_zin)
                sacv(vector)
                fwd_sub(vector)
                back_sub(vector)
                sacv(vector)
                iv = vector.tensor_copy(
                    qdve[:].rearrange("q (r a) -> q r a", a=64),
                    A3[:, :, 64:CSOL].transpose([0, 2, 1]))
                p.mark("dve", iv)

            for pas in range(2):
                for r in range(8):
                    @blk.vector
                    def _(vector, pas=pas, r=r):
                        Mr = qdve[:, r * 64:(r + 1) * 64]
                        if r > 0:
                            qk = qdve[:, 0:r * 64].rearrange("q (k a) -> q k a", a=64)
                            mb = Mr.unsqueeze(1).to_broadcast([128, r, 64])
                            t1 = scr[:, 0:r * 64].rearrange("q (k a) -> q k a", a=64)
                            vector.tensor_tensor(t1, qk, mb, MUL)
                            sacv(vector)
                            vector.tensor_reduce(st2[:, 0:r], t1, AX.X, ADD)
                            sacv(vector)
                            sacv(vector)
                            vector.tensor_mul(st2[:, 0:r], st2[:, 0:r],
                                              st2[:, 8:8 + r])
                            sacv(vector)
                            qak = qdve[:, 0:r * 64].rearrange("q (k a) -> q a k", a=64)
                            cbc = st2[:, 0:r].unsqueeze(1).to_broadcast([128, 64, r])
                            t2 = scr[:, 1024:1024 + 64 * r] \
                                .rearrange("q (a k) -> q a k", k=r)
                            vector.tensor_tensor(t2, qak, cbc, MUL)
                            sacv(vector)
                            vector.tensor_reduce(scr[:, 2048:2112], t2, AX.X, ADD)
                            sacv(vector)
                            sacv(vector)
                            vector.tensor_sub(Mr, Mr, scr[:, 2048:2112])
                            sacv(vector)
                        vector.tensor_tensor(scr[:, 2112:2176], Mr, Mr, MUL)
                        sacv(vector)
                        vector.tensor_reduce(st2[:, 0:1], scr[:, 2112:2176], AX.X, ADD)
                        sacv(vector)
                        sacv(vector)
                        iv = vector.reciprocal(st2[:, 8 + r:9 + r], st2[:, 0:1])
                        p.mark("dve", iv)
            @blk.vector
            def _(vector):
                iv = vector.tensor_copy(sac2[:, 1:2], rd[:, 0:1])
                p.mark("dve", iv)
            ev_cgs = p.last("dve")

            if tap_qcgs is not None:
                @blk.gpsimd
                def _(gpsimd, bat=bat, ev_cgs=ev_cgs):
                    p.wait(gpsimd, ev_cgs)
                    p.mark("dgp", gpsimd.dma_start(out=tap_qcgs[bat], in_=qdve[:]), 16)

            for a0 in range(0, 64, 4):
                par = (a0 // 4) % 2

                @blk.tensor
                def _(tensor, a0=a0, par=par, ev_cgs=ev_cgs):
                    if a0 == 0:
                        p.wait(tensor, ev_cgs)
                    p.wait(tensor, p.last("brev"))
                    for n in range(4):
                        a = a0 + n
                        iv = tensor.transpose(ps_br[0:8, par, n * 128:(n + 1) * 128],
                                              qdve[:, a:512:64], idt[:])
                    p.mark("pe", iv)
                eqt = p.last("pe")

                @blk.vector
                def _(vector, a0=a0, par=par, eqt=eqt):
                    p.wait(vector, eqt)
                    for n in range(4):
                        a = a0 + n
                        iv = vector.tensor_copy(qtf[0:8, a:8192:64],
                                                ps_br[0:8, par, n * 128:(n + 1) * 128])
                    p.mark("brev", iv)
            ev_qtf = p.last("brev")


            @blk.vector
            def _(vector, ev_qtf=ev_qtf):
                p.wait(vector, ev_qtf)
                iv = vector.tensor_tensor(
                    qdve[:].rearrange("q (r a) -> q r a", a=64),
                    qdve[:].rearrange("q (r a) -> q r a", a=64),
                    st2[:, 8:16].unsqueeze(2).to_broadcast([128, 8, 64]), MUL)
                p.mark("dve", iv)

            @blk.vector
            def _(vector, ev_cgs=ev_cgs):
                p.wait(vector, ev_cgs)
                vector.tensor_copy(A3[:, :, 64:CSOL],
                                   qdve[:].rearrange("q (r a) -> q a r", a=64))
                sacv(vector)
                sacv(vector)
                fwd_sub(vector)
                back_sub(vector)
                iv = vector.tensor_copy(sac2[:, 1:2], rd[:, 0:1])
                p.mark("dve", iv)
            ev_r3 = p.last("dve")

            ev_rpe = bridge_rhs_out(rpe, ev_r3)
            if tap_rpe is not None:
                @blk.gpsimd
                def _(gpsimd, bat=bat, ev_rpe=ev_rpe):
                    p.wait(gpsimd, ev_rpe)
                    p.mark("dgp", gpsimd.dma_start(out=tap_rpe[bat], in_=rpe[:]), 16)

            for til in range(8):
                for vert in range(2):
                    g = til * 2 + vert

                    @blk.tensor
                    def _(tensor, bat=bat, til=til, vert=vert, g=g,
                          ev_qtf=ev_qtf, ev_rpe=ev_rpe):
                        if g == 0:
                            p.wait(tensor, ev_qtf)
                            p.wait(tensor, ev_rpe)
                        if g == 8:
                            p.wait(tensor, p.last("dgp"))
                        if g >= 1:
                            p.wait(tensor, p.last("fev"))
                        for n in range(8):
                            q = til * 16 + vert * 8 + n
                            iv = tensor.matmul(ps_f[0:8, n * 64:(n + 1) * 64],
                                               rpe[:, q * 8:(q + 1) * 8],
                                               bh[:, (q % 64) * 64:(q % 64 + 1) * 64],
                                               start=True, stop=True)
                        p.mark("pe", iv)
                    ef = p.last("pe")

                    @blk.vector
                    def _(vector, g=g, ef=ef):
                        p.wait(vector, ef)
                        iv = vector.tensor_copy(fgrp[0:8, g % 2, :], ps_f[0:8, :])
                        p.mark("fev", iv)
                    efe = p.last("fev")

                    @blk.tensor
                    def _(tensor, bat=bat, til=til, vert=vert, g=g, efe=efe):
                        p.wait(tensor, efe)
                        if til >= 2 and vert == 0:
                            p.wait(tensor, p.last("wrev"))
                        tp = (0, 64) if vert else (0, 0)
                        for n in range(8):
                            q = til * 16 + vert * 8 + n
                            sl = slice(vert * 64, vert * 64 + 128 - vert * 64) \
                                if False else slice(vert * 64, vert * 64 + 64)
                            iv = tensor.matmul(ps_wr[sl, til % 2, n * 64:(n + 1) * 64],
                                               qtf[0:8, q * 64:(q + 1) * 64],
                                               fgrp[0:8, g % 2, n * 64:(n + 1) * 64],
                                               start=True, stop=True,
                                               tile_position=tp)
                        p.mark("pe", iv)

                    if g == 7:
                        @blk.gpsimd
                        def _(gpsimd, bat=bat):
                            p.wait(gpsimd, p.last("fev"))
                            p.mark("dgp", gpsimd.dma_start(
                                out=bh[:], in_=b_st[bat][:, 4096:8192]), 16)

                ewr = p.last("pe")
                tile = bat * 8 + til
                b_idx, cs = tile // 2, tile % 2
                slab = wr1 if cs else wr0

                @blk.vector
                def _(vector, til=til, ewr=ewr, slab=slab, b_idx=b_idx):
                    p.wait(vector, ewr)
                    iv = vector.tensor_copy(
                        slab[:, (b_idx * 8) * 64:(b_idx * 8) * 64 + 512],
                        ps_wr[:, til % 2, :])
                    p.mark("wrev", iv)
            ev_wrb = p.last("wrev")

            @blk.vector
            def _(vector, bat=bat, ev_wrb=ev_wrb):
                p.wait(vector, ev_wrb)
                b0 = bat * 4
                s0 = (b0 * 8) * 64
                vector.scalar_tensor_tensor(cD[:], wr0[:, s0:s0 + 2048], ALPHA,
                                            wr1[:, s0:s0 + 2048], MUL, ADD)
                sacv(vector)
                iv = vector.tensor_scalar_mul(cD[:], cD[:], ccon[:, 0:1])
                p.mark("dve", iv)
            ev_cd = p.last("dve")

            @blk.gpsimd
            def _(gpsimd, bat=bat, ev_cd=ev_cd):
                p.wait(gpsimd, ev_cd)
                iv = gpsimd.dma_start(out=pfx_cin.ap()[bat], in_=cD[:])
                p.mark("dgp", iv, 16)
                if tap_cd is not None:
                    p.mark("dgp", gpsimd.dma_start(out=tap_cd[bat], in_=cD[:]), 16)
                gpsimd.wait_ge(p.sems["dgp"], p.cnt["dgp"] - (16 if tap_cd is not None else 0))
                cc = gpsimd.collective_compute(
                    "AllGather",
                    mybir.AluOpType.bypass,
                    replica_groups=[list(range(NCORES))],
                    ins=[pfx_cin.ap()[bat].opt()],
                    outs=[arena.ap()[bat].opt()],
                )
                p.mark("cc", cc)
            ev_cddma = p.last("dgp")

            @blk.vector
            def _(vector, ev_cddma=ev_cddma):
                p.wait(vector, ev_cddma)
                vector.tensor_copy(sac2[:, 0:1], rd[:, 0:1])

        if tap_wr is not None:
            @blk.gpsimd
            def _(gpsimd):
                p.wait(gpsimd, p.last("wrev"))
                p.mark("dgp", gpsimd.dma_start(out=tap_wr[0], in_=wr0[:]), 16)
                p.mark("dgp", gpsimd.dma_start(out=tap_wr[1], in_=wr1[:]), 16)

        if stop_after == "p2":
            p2c.close()
            p24c.close()
            _finish(p, blk)
            return p, out


        ev_ag = p.last("cc")

        @blk.vector
        def _(vector, ev_ag=ev_ag):
            vector.tensor_scalar_mul(wr0[:], wr0[:], ccon[:, 2:3])
            iv = vector.tensor_copy(sac2[:, 0:1], rd[:, 0:1])
            p.mark("dve", iv)
        ev_w0s = p.last("dve")

        ar4 = arena.ap()
        for k in range(7):
            for qq in range(4):
                n = k * 4 + qq
                slot = n % 2

                @blk.gpsimd
                def _(gpsimd, k=k, qq=qq, n=n, slot=slot,
                      ev_ag=ev_ag, ev_w0s=ev_w0s):
                    if n == 0:
                        p.wait(gpsimd, ev_ag)
                        p.wait(gpsimd, ev_w0s)
                    if n >= 2:
                        p.wait(gpsimd, Ev(p._sem("dve"), p.cnt["dve"] - 1))
                    p.mark("dgp", gpsimd.dma_start(
                        out=scr[:, slot * 2048:(slot + 1) * 2048],
                        in_=ar4[qq, k * 128:(k + 1) * 128, :]), 16)
                ed = p.last("dgp")

                @blk.vector
                def _(vector, k=k, qq=qq, ed=ed):
                    p.wait(vector, ed)
                    iv = vector.scalar_tensor_tensor(
                        wr0[:, qq * 2048:(qq + 1) * 2048],
                        scr[:, (k * 4 + qq) % 2 * 2048:((k * 4 + qq) % 2 + 1) * 2048],
                        pmask[:, k:k + 1],
                        wr0[:, qq * 2048:(qq + 1) * 2048], MUL, ADD)
                    p.mark("dve", iv)
        ev_acc = p.last("dve")

        @blk.vector
        def _(vector, ev_acc=ev_acc):
            p.wait(vector, ev_acc)
            vector.tensor_scalar_mul(wr0[:], wr0[:], ccon[:, 1:2])
            sacv(vector)
            iv = vector.scalar_tensor_tensor(wr1[:], wr0[:], ALPHA, wr1[:], MUL, ADD)
            p.mark("dve", iv)
        ev_S = p.last("dve")

        if tap_sab is not None:
            @blk.gpsimd
            def _(gpsimd, ev_S=ev_S):
                p.wait(gpsimd, ev_S)
                p.mark("dgp", gpsimd.dma_start(out=tap_sab[0], in_=wr0[:]), 16)
                p.mark("dgp", gpsimd.dma_start(out=tap_sab[1], in_=wr1[:]), 16)

        p2c.close()

        if stop_after == "p3":
            p24c.close()
            _finish(p, blk)
            return p, out

        p4c = ExitStack()
        wo4 = p.sbuf("wo4", [128, NKC, D], ctx=p4c)
        q4 = p.sbuf("q4", [128, 2, NKC, 128], ctx=p4c)
        ofm = p.sbuf("ofm", [128, 2, NKC, 128], ctx=p4c)
        zt4 = p.sbuf("zt4", [128, 2, D], ctx=p4c)
        xt4 = p.sbuf("xt4", [128, 2, D], ctx=p4c)
        att4 = p.sbuf("att4", [128, 2, D], ctx=p4c)
        xm4 = p.sbuf("xm4", [128, 2, D], ctx=p4c)
        z2t = p.sbuf("z2t", [128, 2, D], ctx=p4c)
        z2f4 = p.sbuf("z2f4", [128, 2, NKC, 128], ctx=p4c)
        stat4 = p.sbuf("stat4", [128, 32], ctx=p4c)
        sqs4 = p.sbuf("sqs4", [128, D], ctx=p4c)
        sac4 = p.sbuf("sac4", [128, 4], ctx=p4c)

        ps_o = p.psum("ps_o", [128, 1024], ctx=p4c)
        ps_wo = p.psum("ps_wo", [128, 2, 1024], ctx=p4c)
        ps_t4 = p.psum("ps_t4", [128, 2, 512], ctx=p4c)

        ln4p = make_ln_pair(stat4, sqs4, sac4)
        tap_xmid = p.tap("xmid", [NTILE, 128, D])
        tap_z2f = p.tap("z2f", [NTILE, 128, NKC, 128])

        @blk.sync
        def _(sync):
            for nm in list(p.sems):
                sync.wait_ge(p.sems[nm], p.cnt[nm])
            p.mark("dsy", sync.dma_start(
                out=wo4[:], in_=W["woA"].rearrange("(kc q) d -> q kc d", q=128)), 16)
        ev_wo4 = p.last("dsy")
        ev_S_final = p.last("dve")

        NPAIR = NTILE // 2
        ev_q4 = [None] * NTILE
        ev_oev = [None] * NTILE
        ev_wo_mm = [None] * NTILE
        ev_x4 = [None] * NTILE
        ev_z4 = [None] * NTILE
        ev_z2fst = [None] * NTILE
        ev_omm = [None] * NTILE
        ev_attp = [None] * NPAIR
        ev_lnp1 = [None] * NPAIR
        ev_xmp = [None] * NPAIR
        ev_lnp2 = [None] * NPAIR
        ev_xst = [None] * NPAIR
        ev_xmb = [None] * NPAIR
        ev_tr4p = [None] * NPAIR
        tr_evs = []

        def p4_front(i):
            b_idx, cs = i // 2, i % 2
            slab = wr1 if cs else wr0

            @blk.sync
            def _(sync, i=i):
                if i == 0:
                    p.wait(sync, ev_wo4)
                if i >= 2:
                    p.wait(sync, ev_omm[i - 2])
                p.mark("dsy", sync.dma_start(out=q4[:, i % 2], in_=q_st[i]), 16)
                ev_q4[i] = p.last("dsy")

            @blk.tensor
            def _(tensor, i=i, b_idx=b_idx, slab=slab):
                p.wait(tensor, ev_q4[i])
                if i == 0:
                    p.wait(tensor, ev_S_final)
                if i >= 1:
                    p.wait(tensor, ev_oev[i - 1])
                for h in range(NH):
                    par, ch = h % 2, h // 2
                    fl = b_idx * 8 + ch
                    sl = slice(64 * par, 64 * par + 64)
                    iv = tensor.matmul(
                        ps_o[sl, ch * 128:(ch + 1) * 128],
                        slab[sl, fl * 64:(fl + 1) * 64],
                        q4[sl, i % 2, ch, :],
                        start=True, stop=True,
                        tile_position=(64 * par, 64 * par))
                ev_omm[i] = p.mark("pe", iv)

            @blk.scalar
            def _(scalar, i=i):
                p.wait(scalar, ev_omm[i])
                if i >= 2:
                    p.wait(scalar, ev_wo_mm[i - 2])
                iv = scalar.activation(
                    ofm[:, i % 2], ps_o[:].rearrange("q (c f) -> q c f", f=128),
                    AF.Identity, scale=ATT_SCALE)
                ev_oev[i] = p.mark("sc", iv)

            @blk.tensor
            def _(tensor, i=i):
                p.wait(tensor, ev_oev[i])
                if i >= 2:
                    p.wait(tensor, ev_attp[i // 2 - 1])
                for hf in range(2):
                    for ch in range(NKC):
                        iv = tensor.matmul(
                            ps_wo[:, i % 2, hf * 512:(hf + 1) * 512],
                            ofm[:, i % 2, ch, :],
                            wo4[:, ch, hf * 512:(hf + 1) * 512],
                            start=(ch == 0), stop=(ch == NKC - 1))
                ev_wo_mm[i] = p.mark("pe", iv)

            @blk.sync
            def _(sync, i=i, b_idx=b_idx, cs=cs):
                if i >= 2:
                    p.wait(sync, ev_attp[i // 2 - 1])
                p.mark("dsy", sync.dma_start(out=zt4[:, i % 2], in_=z_st[i]), 16)
                ev_z4[i] = p.last("dsy")
                if i >= 2:
                    p.wait(sync, ev_xmp[i // 2 - 1])
                p.mark("dsy", sync.dma_start(
                    out=xt4[:, i % 2],
                    in_=x_in[b_idx, cs * 128:(cs + 1) * 128, :]), 16)
                ev_x4[i] = p.last("dsy")

        def p4_back_chain(j):
            a, b = 2 * j, 2 * j + 1

            @blk.vector
            def _(vector, j=j, a=a, b=b):
                p.wait(vector, ev_wo_mm[b])
                p.wait(vector, ev_z4[b])
                if j >= 1:
                    p.wait(vector, ev_lnp1[j - 1])
                vector.tensor_add(att4[:, 0], zt4[:, 0], ps_wo[:, 0, :])
                vector.tensor_copy(sac4[:, 0:2], stat4[:, 0:2])
                vector.tensor_add(att4[:, 1], zt4[:, 1], ps_wo[:, 1, :])
                vector.tensor_copy(sac4[:, 0:2], stat4[:, 0:2])
                iv = vector.tensor_copy(sac4[:, 0:2], stat4[:, 0:2])
                ev_attp[j] = p.mark("dve", iv)

            @blk.gpsimd
            def _(gpsimd, j=j):
                p.wait(gpsimd, ev_attp[j])
                p.mark("gnp", gpsimd.engine_nop())
            ev_attb = p.last("gnp")

            ev_lnp1[j] = ln4p(att4[:], xm4[:], 0,
                              gate_v=ev_attb, gate_s=ev_attb,
                              gate_apply=(ev_xst[j - 1] if j >= 1 else None))

            @blk.vector
            def _(vector, j=j, b=b):
                p.wait(vector, ev_lnp1[j])
                p.wait(vector, ev_x4[b])
                vector.tensor_add(xm4[:], xm4[:], xt4[:])
                vector.tensor_copy(sac4[:, 0:2], stat4[:, 0:2])
                iv = vector.tensor_copy(sac4[:, 0:2], stat4[:, 0:2])
                ev_xmp[j] = p.mark("dve", iv)

            @blk.gpsimd
            def _(gpsimd, j=j):
                p.wait(gpsimd, ev_xmp[j])
                p.mark("gnp", gpsimd.engine_nop())
            ev_xmb[j] = p.last("gnp")

            if j >= 1:
                @blk.scalar
                def _(scalar, j=j):
                    p.wait(scalar, ev_tr4p[j - 1])

            ev_lnp2[j] = ln4p(xm4[:], z2t[:], 16,
                              gate_v=ev_xmb[j], gate_s=ev_xmb[j])

            @blk.gpsimd
            def _(gpsimd, j=j, a=a, b=b):
                p.wait(gpsimd, ev_xmb[j])
                for ti, ii in ((0, a), (1, b)):
                    p.mark("dgp", gpsimd.dma_start(out=xmid_st[ii],
                                                   in_=xm4[:, ti]), 16)
                    if tap_xmid is not None:
                        p.mark("dgp", gpsimd.dma_start(out=tap_xmid[ii],
                                                       in_=xm4[:, ti]), 16)
                ev_xst[j] = p.last("dgp")

        def p4_back_tr(j):
            for ti in range(2):
                ii = 2 * j + ti
                tile_evs = []
                for half in range(2):
                    r = ti * 2 + half

                    @blk.tensor
                    def _(tensor, j=j, ti=ti, half=half, r=r):
                        if r == 0:
                            p.wait(tensor, ev_lnp2[j])
                        if len(tr_evs) >= 2:
                            p.wait(tensor, tr_evs[-2])
                        for c4 in range(4):
                            ch = half * 4 + c4
                            iv = tensor.transpose(
                                ps_t4[:, r % 2, c4 * 128:(c4 + 1) * 128],
                                z2t[:, ti, ch * 128:(ch + 1) * 128],
                                idt[:])
                        p.mark("pe", iv)
                    e_tr = p.last("pe")
                    if r == 3:
                        ev_tr4p[j] = e_tr

                    @blk.scalar
                    def _(scalar, j=j, ti=ti, half=half, r=r, e_tr=e_tr):
                        p.wait(scalar, e_tr)
                        if half == 0 and j >= 1:
                            p.wait(scalar, ev_z2fst[2 * (j - 1) + ti])
                        iv = scalar.copy(
                            z2f4[:, ti, half * 4:(half + 1) * 4, :],
                            ps_t4[:, r % 2, :]
                            .rearrange("q (c f) -> q c f", f=128))
                        p.mark("t4ev", iv)
                    tile_evs.append(p.last("t4ev"))
                    tr_evs.append(tile_evs[-1])

                @blk.gpsimd
                def _(gpsimd, ii=ii, ev2=tile_evs[-1], ti=ti):
                    p.wait(gpsimd, ev2)
                    p.mark("dgp", gpsimd.dma_start(out=z2f_st[ii],
                                                   in_=z2f4[:, ti]), 16)
                    ev_z2fst[ii] = p.last("dgp")
                    if tap_z2f is not None:
                        p.mark("dgp", gpsimd.dma_start(out=tap_z2f[ii],
                                                       in_=z2f4[:, ti]), 16)
                        ev_z2fst[ii] = p.last("dgp")

        for j in range(NPAIR + 1):
            if j >= 1:
                p4_back_chain(j - 1)
            if j < NPAIR:
                p4_front(2 * j)
                p4_front(2 * j + 1)
            if j >= 1:
                p4_back_tr(j - 1)

        p4c.close()
        p24c.close()

        if stop_after == "p4":
            _finish(p, blk)
            return p, out

        p5c = ExitStack()
        cfm = p.sbuf("cfm", [128, 2, NKC, SLOC], ctx=p5c)
        ctm = p.sbuf("ctm", [128, 2, D], ctx=p5c)
        gs5 = p.sbuf("gs5", [128, 3, D], ctx=p5c)
        sac5 = p.sbuf("sac5", [128, 4], ctx=p5c)
        tscr = p.sbuf("tscr", [128, D], ctx=p5c)
        ps5 = p.psum("ps5", [128, 4096], ctx=p5c)
        ps5r = ps5[:].rearrange("q (n s) -> q n s", s=SLOC)

        p4b = ExitStack()
        wf5 = p.sbuf("wf5", [128, NKC, D], ctx=p4b)
        wi5 = p.sbuf("wi5", [128, NKC, D], ctx=p4b)
        wc5 = p.sbuf("wc5", [128, NKC, D], ctx=p4b)
        btm = p.sbuf("btm", [128, 3, D], ctx=p4b)
        zin = p.sbuf("zin", [128, 2, NKC, 128], ctx=p4b)
        gzsb = p.sbuf("gzsb", [128, 2, 3, D], ctx=p4b)

        GF = {"f": (0, AF.Sigmoid), "i": (1, AF.Sigmoid), "c": (2, AF.Tanh)}
        wsb = [wf5, wi5, wc5]

        @blk.sync
        def _(sync):
            for nm in list(p.sems):
                sync.wait_ge(p.sems[nm], p.cnt[nm])
            for wn, dst in zip(["wfT", "wiT", "wcT"], wsb):
                p.mark("dsy", sync.dma_start(
                    out=dst[:],
                    in_=W[wn].rearrange("(kc q) d -> q kc d", q=128)), 16)
            p.mark("dsy", sync.dma_start(
                out=btm[:], in_=bias_rows[:, 0:3, :].to_broadcast([128, 3, D])), 16)
            p.mark("dsy", sync.dma_start(
                out=ctm[:], in_=bias_rows[:, 3:4, :].to_broadcast([128, 2, D])), 16)
        ev_w5 = p.last("dsy")

        @blk.vector
        def _(vector):
            p.wait(vector, ev_w5)
            iv = vector.tensor_copy(
                cfm[:, 0],
                biasf[:, 4 * NKC:5 * NKC].unsqueeze(2)
                .to_broadcast([128, NKC, SLOC]))
            p.mark("dve", iv)
        ev_c0 = p.last("dve")

        ev_zin = {}
        ev_gzmm = {}
        ev_gzev = {}
        ev_gzst = {}
        for i in range(NTILE):
            @blk.gpsimd
            def _(gpsimd, i=i):
                p.wait(gpsimd, ev_z2fst[i])
                if i < 2:
                    p.wait(gpsimd, ev_w5)
                else:
                    p.wait(gpsimd, ev_gzmm[i - 2])
                p.mark("dgp", gpsimd.dma_start(out=zin[:, i % 2], in_=z2f_st[i]), 16)
                ev_zin[i] = p.last("dgp")

            @blk.tensor
            def _(tensor, i=i):
                p.wait(tensor, ev_zin[i])
                if i == 0:
                    p.wait(tensor, ev_w5)
                if i >= 1:
                    p.wait(tensor, ev_gzev[(i - 1, 0)])
                    p.wait(tensor, ev_gzev[(i - 1, 1)])
                if i >= 2:
                    p.wait(tensor, ev_gzev[(i - 2, 2)])
                for kc in range(NKC):
                    stat = zin[:, i % 2, kc, :]
                    for gn in range(3):
                        for hf in range(2):
                            b = (6 * i + 2 * gn + hf) % 8
                            iv = tensor.matmul(
                                ps5[:, b * 512:(b + 1) * 512],
                                stat,
                                wsb[gn][:, kc, hf * 512:(hf + 1) * 512],
                                start=(kc == 0), stop=(kc == NKC - 1))
                ev_gzmm[i] = p.mark("pe", iv)

            @blk.vector
            def _(vector, i=i):
                p.wait(vector, ev_gzmm[i])
                if i >= 2:
                    p.wait(vector, ev_gzst[i - 2])
                for gn in range(3):
                    b = (6 * i + 2 * gn) % 8
                    iv = vector.tensor_add(gzsb[:, i % 2, gn, :],
                                           ps5[:, b * 512:(b + 2) * 512],
                                           btm[:, gn, :])
                    ev_gzev[(i, gn)] = p.mark("dve", iv)

            @blk.scalar
            def _(scalar, i=i):
                p.wait(scalar, ev_gzev[(i, 2)])
                p.mark("dsc", scalar.dma_start(out=gz_st[i], in_=gzsb[:, i % 2]), 16)
                ev_gzst[i] = p.last("dsc")

        p4b.close()
        p5u = ExitStack()
        ustr = p.sbuf("ustr", [128, 6, 4, D], ctx=p5u)
        gzin = p.sbuf("gzin", [128, 2, 2, 3, D], ctx=p5u)

        UW = {"f": "ufT", "i": "uiT", "c": "ucT"}
        ev_p4b = (ev_gzmm[NTILE - 1], ev_gzev[(NTILE - 1, 2)], ev_gzst[NTILE - 1])

        ev_cfmT = {-1: ev_c0}
        ev_cupd = {}
        ev_uld = {}
        ev_uuse = {}
        ev_gz = {}
        ev_gzuse = {}
        ev_umm = {}
        ev_badd = {}
        ev_gact5 = {}

        USPLIT = {0: (2, 2, 0), 1: (2, 1, 1), 2: (1, 2, 1)}

        for t in range(B):
            @blk.sync
            def _(sync, t=t):
                if t < 2:
                    for evv in ev_p4b:
                        p.wait(sync, evv)
                else:
                    p.wait(sync, ev_gzuse[t - 2])
                    p.wait(sync, ev_gzst[2 * t + 1])
                for cs in range(2):
                    p.mark("dsy", sync.dma_start(
                        out=gzin[:, t % 2, cs], in_=gz_st[2 * t + cs]), 16)
            ev_gz[t] = p.last("dsy")

            for gn, g in enumerate("fic"):
                for uh in range(2):
                    m = t * 6 + gn * 2 + uh
                    slot = gn * 2 + uh
                    wap = W[UW[g]].rearrange("(kc q) d -> q kc d", q=128)
                    ng, ns, ny = USPLIT[m % 3]
                    evl = []

                    @blk.gpsimd
                    def _(gpsimd, m=m, slot=slot, uh=uh, wap=wap, ng=ng):
                        if m >= 6:
                            p.wait(gpsimd, ev_uuse[m - 6])
                        else:
                            for evv in ev_p4b:
                                p.wait(gpsimd, evv)
                        p.mark("dgp", gpsimd.dma_start(
                            out=ustr[:, slot, 0:ng, :],
                            in_=wap[:, uh * 4:uh * 4 + ng, :]), 16)
                    evl.append(p.last("dgp"))

                    @blk.scalar
                    def _(scalar, m=m, slot=slot, uh=uh, wap=wap, ng=ng, ns=ns):
                        if m >= 6:
                            p.wait(scalar, ev_uuse[m - 6])
                        else:
                            for evv in ev_p4b:
                                p.wait(scalar, evv)
                        p.mark("dsc", scalar.dma_start(
                            out=ustr[:, slot, ng:ng + ns, :],
                            in_=wap[:, uh * 4 + ng:uh * 4 + ng + ns, :]), 16)
                    evl.append(p.last("dsc"))

                    if ny:
                        @blk.sync
                        def _(sync, m=m, slot=slot, uh=uh, wap=wap, ng=ng, ns=ns):
                            if m >= 6:
                                p.wait(sync, ev_uuse[m - 6])
                            p.mark("dsy", sync.dma_start(
                                out=ustr[:, slot, ng + ns:4, :],
                                in_=wap[:, uh * 4 + ng + ns:uh * 4 + 4, :]), 16)
                        evl.append(p.last("dsy"))
                    ev_uld[m] = tuple(evl)

            for sc in range(2):
                for gn, g in enumerate("fic"):
                    @blk.tensor
                    def _(tensor, t=t, sc=sc, gn=gn):
                        m0 = t * 6 + gn * 2
                        for mm_ in (m0, m0 + 1):
                            for evv in ev_uld[mm_]:
                                p.wait(tensor, evv)
                        if sc == 0 and gn == 0:
                            p.wait(tensor, ev_cfmT[t - 1])
                        if (t, sc) == (0, 0):
                            p.wait(tensor, ev_gzev[(NTILE - 2, 2)] if gn == 0
                                   else ev_gzev[(NTILE - 1, gn - 1)])
                        else:
                            p.wait(tensor, ev_badd[(t, 0, gn) if sc == 1
                                                   else (t - 1, 1, gn)])
                        for kc in range(NKC):
                            stat = cfm[:, t % 2, kc, sc * 128:(sc + 1) * 128]
                            for hf in range(2):
                                iv = tensor.matmul(
                                    ps5[:, (2 * gn + hf) * 512:
                                        (2 * gn + hf + 1) * 512],
                                    stat,
                                    ustr[:, gn * 2 + kc // 4, kc % 4,
                                         hf * 512:(hf + 1) * 512],
                                    start=(kc == 0), stop=(kc == NKC - 1))
                        ev_umm[(t, sc, gn)] = p.mark("pe", iv)
                        if sc == 1:
                            ev_uuse[m0] = ev_umm[(t, sc, gn)]
                            ev_uuse[m0 + 1] = ev_umm[(t, sc, gn)]

                    @blk.vector
                    def _(vector, t=t, sc=sc, gn=gn):
                        p.wait(vector, ev_umm[(t, sc, gn)])
                        p.wait(vector, ev_gz[t])
                        if (t, sc, gn) != (0, 0, 0):
                            p.wait(vector, p.last("sc"))
                        vector.tensor_add(
                            tscr[:],
                            ps5[:, 2 * gn * 512:(2 * gn + 2) * 512],
                            gzin[:, t % 2, sc, gn, :])
                        vector.tensor_copy(sac5[:, 0:2], gs5[:, 1, 0:2])
                        iv = vector.tensor_copy(sac5[:, 0:2], gs5[:, 1, 0:2])
                        ev_badd[(t, sc, gn)] = p.mark("dve", iv)
                        if (sc, gn) == (1, 2):
                            ev_gzuse[t] = ev_badd[(t, sc, gn)]

                    @blk.scalar
                    def _(scalar, t=t, sc=sc, gn=gn, g=g):
                        p.wait(scalar, ev_badd[(t, sc, gn)])
                        iv = scalar.activation(gs5[:, gn, :], tscr[:], GF[g][1])
                        ev_gact5[(t, sc, gn)] = p.mark("sc", iv)

                @blk.vector
                def _(vector, t=t, sc=sc):
                    p.wait(vector, ev_gact5[(t, sc, 2)])
                    cc = ctm[:, sc, :]
                    vector.tensor_tensor(cc, gs5[:, 0, :], cc, MUL)
                    vector.tensor_mul(gs5[:, 1, :], gs5[:, 1, :], gs5[:, 2, :])
                    vector.tensor_add(cc, cc, gs5[:, 1, :])
                    vector.tensor_copy(sac5[:, 2:4], gs5[:, 0, 0:2])
                    iv = vector.tensor_copy(sac5[:, 2:4], gs5[:, 0, 0:2])
                    ev_cupd[(t, sc)] = p.mark("dve", iv)

            for rnd in range(4):
                sc, chh = rnd // 2, (rnd % 2) * 4
                bnk = 6 + rnd % 2

                @blk.tensor
                def _(tensor, t=t, sc=sc, chh=chh, bnk=bnk, rnd=rnd):
                    p.wait(tensor, ev_cupd[(t, sc)])
                    if (t, rnd) == (0, 0):
                        p.wait(tensor, ev_gzev[(NTILE - 1, 2)])
                    else:
                        p.wait(tensor, p.last("c5ev"))
                    for c4 in range(4):
                        ch = chh + c4
                        iv = tensor.transpose(
                            ps5[:, bnk * 512 + c4 * 128:bnk * 512 + (c4 + 1) * 128],
                            ctm[:, sc, ch * 128:(ch + 1) * 128],
                            idt[:])
                    p.mark("pe", iv)
                etr = p.last("pe")

                @blk.vector
                def _(vector, t=t, sc=sc, chh=chh, bnk=bnk, etr=etr):
                    p.wait(vector, etr)
                    iv = vector.tensor_copy(
                        cfm[:, (t + 1) % 2, chh:chh + 4, sc * 128:(sc + 1) * 128],
                        ps5[:, bnk * 512:(bnk + 1) * 512]
                        .rearrange("q (c f) -> q c f", f=128))
                    p.mark("c5ev", iv)
            ev_cfmT[t] = p.last("c5ev")

        ev_cfinal = ev_cfmT[B - 1]
        cfin = cfm[:, B % 2]
        ev_ulast = ev_umm[(B - 1, 1, 2)]

        tap_cf = p.tap("cfin", [128, NKC, SLOC])
        if tap_cf is not None:
            @blk.gpsimd
            def _(gpsimd):
                p.wait(gpsimd, ev_cfinal)
                p.mark("dgp", gpsimd.dma_start(out=tap_cf, in_=cfin), 16)

        if stop_after == "p5":
            p5u.close()
            p5c.close()
            _finish(p, blk)
            return p, out

        p5u.close()
        uo6 = p.sbuf("uo6", [128, NKC, D], ctx=p5c)
        wo6 = p.sbuf("wo6", [128, NKC, D], ctx=p5c)
        go6 = p.sbuf("go6", [128, NKC, SLOC], ctx=p5c)
        cctm = p.sbuf("cctm", [128, 2, D], ctx=p5c)
        z2f6 = p.sbuf("z2f6", [128, 2, NKC, 128], ctx=p5c)
        xm6 = p.sbuf("xm6", [128, 2, D], ctx=p5c)
        hpre = p.sbuf("hpre", [128, 2, D], ctx=p5c)
        ht = p.sbuf("ht", [128, D], ctx=p5c)
        lnout = p.sbuf("lnout", [128, D], ctx=p5c)
        stat6 = p.sbuf("stat6", [128, 16], ctx=p5c)
        sqs6 = p.sbuf("sqs6", [128, D], ctx=p5c)
        sac6 = p.sbuf("sac6", [128, 4], ctx=p5c)
        ln6 = make_ln(stat6, sqs6, sac6)

        @blk.sync
        def _(sync):
            p.wait(sync, ev_ulast)
            p.wait(sync, ev_gzuse[B - 1])
            p.mark("dsy", sync.dma_start(
                out=uo6[:], in_=W["uoT"].rearrange("(kc q) d -> q kc d", q=128)), 16)
        ev_uo = p.last("dsy")

        @blk.tensor
        def _(tensor, ev_uo=ev_uo):
            p.wait(tensor, ev_uo)
            p.wait(tensor, ev_cfinal)
            for ch in range(NKC):
                for kc in range(NKC):
                    iv = tensor.matmul(ps5r[:, ch, :],
                                       uo6[:, kc, ch * 128:(ch + 1) * 128],
                                       cfin[:, kc, :],
                                       start=(kc == 0), stop=(kc == NKC - 1))
            p.mark("pe", iv)
        ev_ccmm = p.last("pe")

        @blk.scalar
        def _(scalar, ev_ccmm=ev_ccmm):
            p.wait(scalar, ev_ccmm)
            for ch in range(NKC):
                iv = scalar.activation(go6[:, ch, :], ps5r[:, ch, :],
                                       AF.Identity,
                                       bias=biasf[:, 3 * NKC + ch:3 * NKC + ch + 1])
            p.mark("sc", iv)
        ev_ccf = p.last("sc")

        @blk.sync
        def _(sync):
            p.mark("dsy", sync.dma_start(
                out=wo6[:], in_=W["woT"].rearrange("(kc q) d -> q kc d", q=128)), 16)
        ev_wo6 = p.last("dsy")

        for cs in range(2):
            for half in range(2):
                @blk.tensor
                def _(tensor, cs=cs, half=half, ev_ccf=ev_ccf):
                    if cs == 0 and half == 0:
                        p.wait(tensor, ev_ccf)
                    p.wait(tensor, p.last("t6ev"))
                    for c4 in range(4):
                        ch = half * 4 + c4
                        iv = tensor.transpose(
                            ps5[:, 2048 + c4 * 128:2048 + (c4 + 1) * 128],
                            go6[:, ch, cs * 128:(cs + 1) * 128],
                            idt[:])
                    p.mark("pe", iv)
                e6 = p.last("pe")

                @blk.scalar
                def _(scalar, cs=cs, half=half, e6=e6):
                    p.wait(scalar, e6)
                    iv = scalar.copy(cctm[:, cs, half * 512:(half + 1) * 512],
                                     ps5[:, 2048:2560])
                    p.mark("t6ev", iv)
        ev_cctm = p.last("t6ev")

        ev_hz = [None] * NTILE
        ev_hx = [None] * NTILE
        ev_hmm = [None] * NTILE
        ev_hpre = [None] * NTILE
        ev_tanh = [None] * NTILE
        ev_fin = [None] * NTILE
        ev_outst = [None] * NTILE

        def p6_front(i):
            b_idx, cs = i // 2, i % 2

            @blk.sync
            def _(sync, i=i):
                if i == 0:
                    p.wait(sync, ev_wo6)
                if i >= 2:
                    p.wait(sync, ev_hmm[i - 2])
                p.mark("dsy", sync.dma_start(out=z2f6[:, i % 2], in_=z2f_st[i]), 16)
                ev_hz[i] = p.last("dsy")
                if i >= 2:
                    p.wait(sync, ev_fin[i - 2])
                p.mark("dsy", sync.dma_start(out=xm6[:, i % 2], in_=xmid_st[i]), 16)
                ev_hx[i] = p.last("dsy")

            @blk.tensor
            def _(tensor, i=i):
                p.wait(tensor, ev_hz[i])
                if i == 0:
                    p.wait(tensor, ev_cctm)
                if i >= 2:
                    p.wait(tensor, ev_hpre[i - 2])
                base = 2048 + (i % 2) * 1024
                for hf in range(2):
                    for ch in range(NKC):
                        iv = tensor.matmul(
                            ps5[:, base + hf * 512:base + (hf + 1) * 512],
                            z2f6[:, i % 2, ch, :],
                            wo6[:, ch, hf * 512:(hf + 1) * 512],
                            start=(ch == 0), stop=(ch == NKC - 1))
                ev_hmm[i] = p.mark("pe", iv)

            @blk.vector
            def _(vector, i=i, cs=cs):
                p.wait(vector, ev_hmm[i])
                if i >= 2:
                    p.wait(vector, ev_tanh[i - 2])
                base = 2048 + (i % 2) * 1024
                iv = vector.tensor_add(hpre[:, i % 2], cctm[:, cs, :],
                                       ps5[:, base:base + 1024])
                ev_hpre[i] = p.mark("dve", iv)

        def p6_back(i):
            b_idx, cs = i // 2, i % 2

            @blk.scalar
            def _(scalar, i=i):
                p.wait(scalar, ev_hpre[i])
                iv = scalar.activation(ht[:], hpre[:, i % 2], AF.Tanh)
                ev_tanh[i] = p.mark("sc", iv)

            e_ln4 = ln6(i, ht[:], lnout[:], 0, gate_v=ev_tanh[i])

            @blk.vector
            def _(vector, i=i, e_ln4=e_ln4):
                p.wait(vector, e_ln4)
                p.wait(vector, ev_hx[i])
                iv = vector.tensor_add(lnout[:], lnout[:], xm6[:, i % 2])
                ev_fin[i] = p.mark("dve", iv)

            @blk.gpsimd
            def _(gpsimd, i=i, b_idx=b_idx, cs=cs):
                p.wait(gpsimd, ev_fin[i])
                p.mark("dgp", gpsimd.dma_start(
                    out=out[b_idx, cs * 128:(cs + 1) * 128, :], in_=lnout[:]), 16)
                ev_outst[i] = p.last("dgp")

            @blk.scalar
            def _(scalar, i=i):
                p.wait(scalar, ev_outst[i])

        for i in range(NTILE + 1):
            if i < NTILE:
                p6_front(i)
            if i >= 1:
                p6_back(i - 1)

        p5c.close()
        _finish(p, blk)
        return p, out


def _finish(p, blk):
    @blk.sync
    def _(sync):
        for nm in list(p.sems):
            sync.wait_ge(p.sems[nm], p.cnt[nm])



def prep_inputs(inputs, core):
    x = inputs["L0_attn_x"]
    f32 = lambda a: np.ascontiguousarray(a, dtype=np.float32)
    d = {
        "x_in": f32(x[:, core * SLOC:(core + 1) * SLOC, :]),
        "wq": f32(inputs["L0_attn_Wq"]),
        "wk": f32(inputs["L0_attn_Wk"]),
        "wv": f32(inputs["L0_attn_Wv"]),
        "woA": f32(inputs["L0_attn_Wo"]),
        "wfT": f32(np.asarray(inputs["L0_ffn_wf"]).T),
        "ufT": f32(np.asarray(inputs["L0_ffn_uf"]).T),
        "wiT": f32(np.asarray(inputs["L0_ffn_wi"]).T),
        "uiT": f32(np.asarray(inputs["L0_ffn_ui"]).T),
        "wcT": f32(np.asarray(inputs["L0_ffn_wc"]).T),
        "ucT": f32(np.asarray(inputs["L0_ffn_uc"]).T),
        "woT": f32(np.asarray(inputs["L0_ffn_wo"]).T),
        "uoT": f32(np.asarray(inputs["L0_ffn_uo"]).T),
        "ln_g": f32(np.asarray(inputs["L0_attn_ln_g"]).reshape(1, D)),
        "ln_b": f32(np.asarray(inputs["L0_attn_ln_b"]).reshape(1, D)),
        "ident_in": np.eye(128, dtype=np.float32),
    }
    bf = np.stack([np.asarray(inputs["L0_ffn_bf"]), np.asarray(inputs["L0_ffn_bi"]),
                   np.asarray(inputs["L0_ffn_bc"]), np.asarray(inputs["L0_ffn_bo"]),
                   np.asarray(inputs["L0_ffn_in_c"])])
    d["bias_f"] = f32(bf.reshape(5, NKC, 128).transpose(2, 0, 1).reshape(128, 5 * NKC))
    d["bias_rows"] = f32(np.stack(
        [np.asarray(inputs["L0_ffn_bf"]), np.asarray(inputs["L0_ffn_bi"]),
         np.asarray(inputs["L0_ffn_bc"]), np.asarray(inputs["L0_ffn_in_c"])]
    ).reshape(1, 4, D))
    cc = np.zeros((128, 4), dtype=np.float32)
    cc[:, 0] = ALPHA ** (-2.0 * core)
    cc[:, 1] = ALPHA ** (2.0 * core - 1.0)
    cc[:, 2] = ALPHA ** (1.0 - 2.0 * core)
    d["cconst"] = cc
    idxs = np.full((1, 8), 8, dtype=np.uint32)
    for k in range(7):
        if k < core:
            idxs[0, k] = k
    d["pfx_idx"] = idxs
    pm = np.zeros((128, 8), dtype=np.float32)
    pm[:, 0:core] = 1.0
    d["pfx_mask"] = pm
    return d


def kernel(**inputs):
    p, _ = build()
    in_maps = [prep_inputs(inputs, c) for c in range(NCORES)]
    res = run_bass_kernel_spmd(p.nc, in_maps, core_ids=list(range(NCORES)))
    return np.concatenate([res.results[c]["out"] for c in range(NCORES)],
                          axis=1).astype(np.float32)
